# version 01
# baseline (speedup 1.0000x reference)
import numpy as np
import ml_dtypes
import jax
import concourse.bass as bass
import concourse.mybir as mybir
from concourse.bass_utils import run_bass_kernel_spmd

F32 = mybir.dt.float32
BF16 = mybir.dt.bfloat16
AF = mybir.ActivationFunctionType
ALU = mybir.AluOpType
AX = mybir.AxisListType

B_, S_, E_, H_ = 2, 8192, 1024, 16
HD_ = 64
NCORE = 8
T_ = 2048
NCH = 8
MAGIC = 12582912.0


class Prog:

    def __init__(self, nc):
        self.nc = nc
        self.items = []

    def add(self, eng, fn, inc=1):
        self.items.append((eng, fn, inc))

    def dma(self, eng, out, in_):
        self.items.append(
            (eng, (lambda e, o=out, i=in_: e.dma_start(out=o, in_=i)), 16))

    def emit(self, block, sem):
        waits, tot = [], 0
        for _, _, inc in self.items:
            waits.append(tot)
            tot += inc
        used = sorted({en for en, _, _ in self.items})

        def mk(engname):
            def emitter(eng):
                for (en, fn, inc), w in zip(self.items, waits):
                    if en != engname:
                        continue
                    eng.wait_ge(sem, w)
                    fn(eng).then_inc(sem, inc)
                if engname == used[0]:
                    eng.wait_ge(sem, tot)
            return emitter
        for name in used:
            getattr(block, name)(mk(name))


def _ap(t, off, dims):
    return bass.AP(t, off, dims)


DEBUG_A = False


def build_A():
    nc = bass.Bass("TRN2", target_bir_lowering=False)
    xT = nc.dram_tensor("xT", [E_, T_], F32, kind="ExternalInput")
    wqT = nc.dram_tensor("wqT", [E_, E_], F32, kind="ExternalInput")
    wkT = nc.dram_tensor("wkT", [E_, E_], F32, kind="ExternalInput")
    wvT = nc.dram_tensor("wvT", [E_, E_], F32, kind="ExternalInput")
    selT = nc.dram_tensor("selT", [E_, 64], F32, kind="ExternalInput")
    keys = nc.dram_tensor("keys", [64, E_, 128], F32, kind="ExternalInput")
    values = nc.dram_tensor("values", [64, 128, E_], F32, kind="ExternalInput")
    cosL = nc.dram_tensor("cosL", [T_, 32], F32, kind="ExternalInput")
    sinL = nc.dram_tensor("sinL", [T_, 32], F32, kind="ExternalInput")
    idf = nc.dram_tensor("idf", [128, 128], F32, kind="ExternalInput")
    onesr = nc.dram_tensor("onesr", [1, 128], F32, kind="ExternalInput")
    onesc = nc.dram_tensor("onesc", [128, 1], F32, kind="ExternalInput")

    qT_o = nc.dram_tensor("qT_o", [E_, T_], BF16, kind="ExternalOutput")
    kT_o = nc.dram_tensor("kT_o", [E_, T_], BF16, kind="ExternalOutput")
    v_o = nc.dram_tensor("v_o", [T_, E_], BF16, kind="ExternalOutput")
    g_o = nc.dram_tensor("g_o", [E_, T_], F32, kind="ExternalOutput")
    hg_d = nc.dram_tensor("hg_d", [T_, 64 * 128], BF16,
                          kind="ExternalOutput" if DEBUG_A else "Internal")

    from contextlib import ExitStack
    es = ExitStack()
    with es:
        xn_bf = es.enter_context(nc.sbuf_tensor("xn_bf", [128, NCH * T_], BF16))
        gates = es.enter_context(nc.sbuf_tensor("gates", [128, 16 * 64], F32))
        id_f = es.enter_context(nc.sbuf_tensor("id_f", [128, 128], F32))
        id_bf = es.enter_context(nc.sbuf_tensor("id_bf", [128, 128], BF16))
        ones_r = es.enter_context(nc.sbuf_tensor("ones_r", [1, 128], F32))
        ones_c = es.enter_context(nc.sbuf_tensor("ones_c", [128, 1], F32))
        PSA = es.enter_context(nc.psum_tensor("PSA", [128, 2048], F32))
        PSB = es.enter_context(nc.psum_tensor("PSB", [128, 1024], F32))
        PSC = es.enter_context(nc.psum_tensor("PSC", [128, 512], F32))
        PST = es.enter_context(nc.psum_tensor("PST", [128, 1024], BF16))

        sem = es.enter_context(nc.semaphore("step"))
        block = es.enter_context(nc.Block())
        p = Prog(nc)

        es1 = ExitStack()
        es1.__enter__()
        selT_s = es1.enter_context(nc.sbuf_tensor("selT_s", [128, NCH * 64], F32))
        mu_rep = es1.enter_context(nc.sbuf_tensor("mu_rep", [128, T_], F32))
        rs_rep = es1.enter_context(nc.sbuf_tensor("rs_rep", [128, T_], F32))
        xch = es1.enter_context(nc.sbuf_tensor("xch", [128, T_], F32))
        sq = es1.enter_context(nc.sbuf_tensor("sq", [128, T_], F32))
        row_a = es1.enter_context(nc.sbuf_tensor("row_a", [1, T_], F32))
        row_b = es1.enter_context(nc.sbuf_tensor("row_b", [1, T_], F32))
        row_c = es1.enter_context(nc.sbuf_tensor("row_c", [1, 512], F32))
        row_d = es1.enter_context(nc.sbuf_tensor("row_d", [1, 512], F32))
        sels = es1.enter_context(nc.sbuf_tensor("sels", [64, T_], F32))
        seltm = es1.enter_context(nc.sbuf_tensor("seltm", [128, 64], F32))
        wk1 = es1.enter_context(nc.sbuf_tensor("wk1", [128, 64], F32))
        wk2 = es1.enter_context(nc.sbuf_tensor("wk2", [128, 64], F32))
        mcol = es1.enter_context(nc.sbuf_tensor("mcol", [128, 1], F32))

        p.dma("sync", id_f[:, :], idf[:, :])
        p.dma("gpsimd", id_bf[:, :], idf[:, :])
        p.dma("sync", ones_r[:, :], onesr[:, :])
        p.dma("sync", ones_c[:, :], onesc[:, :])
        for c in range(NCH):
            p.dma("sync", selT_s[:, c * 64:(c + 1) * 64],
                  selT[c * 128:(c + 1) * 128, :])

        for ts in range(4):
            for c in range(NCH):
                p.dma("sync", xch[:, 0:512],
                      xT[c * 128:(c + 1) * 128, ts * 512:(ts + 1) * 512])
                p.add("scalar", lambda e, c=c: e.activation(
                    sq[:, 0:512], xch[:, 0:512], AF.Square))
                p.add("tensor", lambda e, c=c: e.matmul(
                    PSC[0:1, :], ones_c[:, :], xch[:, 0:512],
                    start=(c == 0), stop=(c == NCH - 1)))
                p.add("tensor", lambda e, c=c: e.matmul(
                    PSB[0:1, 0:512], ones_c[:, :], sq[:, 0:512],
                    start=(c == 0), stop=(c == NCH - 1)))
            sl = slice(ts * 512, (ts + 1) * 512)
            p.add("vector", lambda e, sl=sl: e.tensor_scalar_mul(
                row_a[:, sl], PSC[0:1, :], 1.0 / 1024.0))
            p.add("vector", lambda e: e.tensor_scalar_mul(
                row_c[:, :], PSB[0:1, 0:512], 1.0 / 1024.0))
            p.add("vector", lambda e, sl=sl: e.tensor_tensor(
                row_d[:, :], row_a[:, sl], row_a[:, sl], ALU.mult))
            p.add("vector", lambda e: e.tensor_tensor(
                row_c[:, :], row_c[:, :], row_d[:, :], ALU.subtract))
            p.add("vector", lambda e: e.tensor_scalar_add(
                row_c[:, :], row_c[:, :], 1e-5))
            p.add("scalar", lambda e: e.activation(
                row_d[:, :], row_c[:, :], AF.Sqrt))
            p.add("vector", lambda e, sl=sl: e.reciprocal(row_b[:, sl], row_d[:, :]))
            p.add("tensor", lambda e, sl=sl: e.matmul(
                PSB[0:128, 0:512], ones_r[:, :], row_a[:, sl], start=True, stop=True))
            p.add("vector", lambda e, sl=sl: e.tensor_copy(
                mu_rep[:, sl], PSB[0:128, 0:512]))
            p.add("tensor", lambda e, sl=sl: e.matmul(
                PSC[0:128, :], ones_r[:, :], row_b[:, sl], start=True, stop=True))
            p.add("vector", lambda e, sl=sl: e.tensor_copy(
                rs_rep[:, sl], PSC[0:128, :]))

        for c in range(NCH):
            p.dma("sync", xch[:, :], xT[c * 128:(c + 1) * 128, :])
            p.add("vector", lambda e: e.tensor_tensor(
                sq[:, :], xch[:, :], mu_rep[:, :], ALU.subtract))
            p.add("vector", lambda e: e.tensor_tensor(
                xch[:, :], sq[:, :], rs_rep[:, :], ALU.mult))
            p.add("vector", lambda e, c=c: e.tensor_copy(
                xn_bf[:, c * T_:(c + 1) * T_], xch[:, :]))
            for ts in range(4):
                p.add("tensor", lambda e, c=c, ts=ts: e.matmul(
                    PSA[0:64, ts * 512:(ts + 1) * 512],
                    selT_s[:, c * 64:(c + 1) * 64],
                    xch[:, ts * 512:(ts + 1) * 512],
                    start=(c == 0), stop=(c == NCH - 1)))

        p.add("scalar", lambda e: e.activation(sels[:, :], PSA[0:64, :], AF.Sigmoid))
        for tt in range(16):
            p.add("tensor", lambda e, tt=tt: e.transpose(
                PSB[0:128, 0:64], sels[0:64, tt * 128:(tt + 1) * 128], id_f[0:64, 0:64]))
            p.add("vector", lambda e: e.tensor_copy(seltm[:, :], PSB[0:128, 0:64]))
            p.add("vector", lambda e: e.tensor_copy(wk1[:, :], seltm[:, :]))
            for r in range(4):
                src = wk1 if r % 2 == 0 else wk2
                dst = wk2 if r % 2 == 0 else wk1
                p.add("vector", lambda e, src=src: e.reduce_max(
                    mcol[:, :], src[:, :], AX.X))
                if r < 3:
                    mb = _ap(mcol, 0, [[1, 128], [0, 64]])
                    p.add("vector", lambda e, src=src, dst=dst, mb=mb: e.tensor_tensor(
                        dst[:, :], src[:, :], mb, ALU.is_equal))
                    p.add("vector", lambda e, src=src, dst=dst: e.scalar_tensor_tensor(
                        dst[:, :], dst[:, :], -1e30, src[:, :], ALU.mult, ALU.add))
            mb = _ap(mcol, 0, [[1, 128], [0, 64]])
            p.add("vector", lambda e, mb=mb: e.tensor_tensor(
                wk1[:, :], seltm[:, :], mb, ALU.is_ge))
            p.add("vector", lambda e, tt=tt: e.tensor_tensor(
                gates[:, tt * 64:(tt + 1) * 64], seltm[:, :], wk1[:, :], ALU.mult))
        if DEBUG_A:
            sels_o = nc.dram_tensor("sels_o", [64, T_], F32, kind="ExternalOutput")
            gates_o = nc.dram_tensor("gates_o", [128, 1024], F32, kind="ExternalOutput")
            p.dma("sync", sels_o[:, :], sels[:, :])
            p.dma("sync", gates_o[:, :], gates[:, :])
        es1.__exit__(None, None, None)

        es2 = ExitStack()
        with es2:
            W_sb = es2.enter_context(nc.sbuf_tensor("W_sb", [128, 3 * NCH * 1024], BF16))
            qT_sb = es2.enter_context(nc.sbuf_tensor("qT_sb", [128, NCH * T_], BF16))
            kT_sb = es2.enter_context(nc.sbuf_tensor("kT_sb", [128, NCH * T_], BF16))
            qf = es2.enter_context(nc.sbuf_tensor("qf", [128, 1024], F32))
            qr = es2.enter_context(nc.sbuf_tensor("qr", [128, 1024], F32))
            qr_bf = es2.enter_context(nc.sbuf_tensor("qr_bf", [128, 1024], BF16))
            t1s = es2.enter_context(nc.sbuf_tensor("t1s", [128, 512], F32))
            t2s = es2.enter_context(nc.sbuf_tensor("t2s", [128, 512], F32))
            ct = es2.enter_context(nc.sbuf_tensor("ct", [128, 32], F32))
            st = es2.enter_context(nc.sbuf_tensor("st", [128, 32], F32))
            vstage = es2.enter_context(nc.sbuf_tensor("vstage", [128, 1024], BF16))

            for w_i, wsrc in enumerate((wqT, wkT, wvT)):
                for c in range(NCH):
                    p.dma("gpsimd", W_sb[:, (w_i * NCH + c) * 1024:(w_i * NCH + c + 1) * 1024],
                          wsrc[c * 128:(c + 1) * 128, :])

            cosap = _ap(ct, 0, [[32, 128], [0, 16], [1, 32]])
            sinap = _ap(st, 0, [[32, 128], [0, 16], [1, 32]])

            def rope(p, src, dst):
                ev = lambda t, o=0: _ap(t, o, [[1024, 128], [64, 16], [2, 32]])
                p.add("vector", lambda e: e.tensor_tensor(t1s[:, :], ev(src), cosap, ALU.mult))
                p.add("vector", lambda e: e.tensor_tensor(t2s[:, :], ev(src, 1), sinap, ALU.mult))
                p.add("vector", lambda e: e.tensor_tensor(ev(dst), t1s[:, :], t2s[:, :], ALU.subtract))
                p.add("vector", lambda e: e.tensor_tensor(t1s[:, :], ev(src), sinap, ALU.mult))
                p.add("vector", lambda e: e.tensor_tensor(t2s[:, :], ev(src, 1), cosap, ALU.mult))
                p.add("vector", lambda e: e.tensor_tensor(ev(dst, 1), t1s[:, :], t2s[:, :], ALU.add))

            for tt in range(16):
                tsl = slice(tt * 128, (tt + 1) * 128)
                p.dma("sync", ct[:, :], cosL[tsl, :])
                p.dma("sync", st[:, :], sinL[tsl, :])
                for w_i, dstT in ((0, qT_sb), (1, kT_sb)):
                    for os in range(2):
                        for c in range(NCH):
                            p.add("tensor", lambda e, w_i=w_i, os=os, c=c, tt=tt: e.matmul(
                                PSA[0:128, w_i * 1024 + os * 512: w_i * 1024 + (os + 1) * 512],
                                xn_bf[:, c * T_ + tt * 128: c * T_ + (tt + 1) * 128],
                                W_sb[:, (w_i * NCH + c) * 1024 + os * 512:
                                        (w_i * NCH + c) * 1024 + (os + 1) * 512],
                                start=(c == 0), stop=(c == NCH - 1)))
                    p.add("scalar", lambda e, w_i=w_i: e.activation(
                        qf[:, :], PSA[0:128, w_i * 1024:(w_i + 1) * 1024], AF.Copy))
                    rope(p, qf, qr)
                    p.add("vector", lambda e: e.tensor_copy(qr_bf[:, :], qr[:, :]))
                    for c8 in range(NCH):
                        p.add("tensor", lambda e, c8=c8: e.transpose(
                            PST[:, c8 * 128:(c8 + 1) * 128],
                            qr_bf[:, c8 * 128:(c8 + 1) * 128], id_bf[:, :]))
                    for c8 in range(NCH):
                        p.add("vector", lambda e, c8=c8, tt=tt, dstT=dstT: e.tensor_copy(
                            dstT[:, c8 * T_ + tt * 128: c8 * T_ + (tt + 1) * 128],
                            PST[:, c8 * 128:(c8 + 1) * 128]))
                for os in range(2):
                    for c in range(NCH):
                        p.add("tensor", lambda e, os=os, c=c, tt=tt: e.matmul(
                            PSB[0:128, os * 512:(os + 1) * 512],
                            xn_bf[:, c * T_ + tt * 128: c * T_ + (tt + 1) * 128],
                            W_sb[:, (2 * NCH + c) * 1024 + os * 512:
                                    (2 * NCH + c) * 1024 + (os + 1) * 512],
                            start=(c == 0), stop=(c == NCH - 1)))
                p.add("vector", lambda e: e.tensor_copy(vstage[:, :], PSB[0:128, :]))
                p.dma("sync", v_o[tsl, :], vstage[:, :])
            p.dma("sync",
                  _ap(qT_o, 0, [[T_, 128], [128 * T_, NCH], [1, T_]]),
                  _ap(qT_sb, 0, [[NCH * T_, 128], [T_, NCH], [1, T_]]))
            p.dma("sync",
                  _ap(kT_o, 0, [[T_, 128], [128 * T_, NCH], [1, T_]]),
                  _ap(kT_sb, 0, [[NCH * T_, 128], [T_, NCH], [1, T_]]))

        es3 = ExitStack()
        with es3:
            keys4 = es3.enter_context(nc.sbuf_tensor("keys4", [128, NCH * 512], BF16))
            hg_bf = es3.enter_context(nc.sbuf_tensor("hg_bf", [128, 512], BF16))
            for kg in range(16):
                for c in range(NCH):
                    p.dma("gpsimd", keys4[:, c * 512:(c + 1) * 512],
                          _ap(keys, (4 * kg) * E_ * 128 + c * 128 * 128,
                              [[128, 128], [E_ * 128, 4], [1, 128]]))
                for tt in range(16):
                    for c in range(NCH):
                        p.add("tensor", lambda e, c=c, tt=tt: e.matmul(
                            PSC[:, :],
                            xn_bf[:, c * T_ + tt * 128: c * T_ + (tt + 1) * 128],
                            keys4[:, c * 512:(c + 1) * 512],
                            start=(c == 0), stop=(c == NCH - 1)))
                    gap = _ap(gates, tt * 64 + kg * 4, [[16 * 64, 128], [1, 4], [0, 128]])
                    p.add("vector", lambda e, gap=gap: e.scalar_tensor_tensor(
                        hg_bf[:, :], PSC[:, :], 0.0, gap, ALU.max, ALU.mult))
                    p.dma("sync", hg_d[tt * 128:(tt + 1) * 128, kg * 512:(kg + 1) * 512],
                          hg_bf[:, :])

        es4 = ExitStack()
        with es4:
            vals = es4.enter_context(nc.sbuf_tensor("vals", [128, 64 * 1024], BF16))
            hgrow = es4.enter_context(nc.sbuf_tensor("hgrow", [128, 64 * 128], BF16))
            gsT = es4.enter_context(nc.sbuf_tensor("gsT", [128, 512], BF16))
            gstage = es4.enter_context(nc.sbuf_tensor("gstage", [128, 1024], F32))
            for k in range(64):
                p.dma("gpsimd", vals[:, k * 1024:(k + 1) * 1024],
                      _ap(values, k * 128 * E_, [[E_, 128], [1, E_]]))
            for tt in range(16):
                p.dma("sync", hgrow[:, :], hg_d[tt * 128:(tt + 1) * 128, :])
                for kg in range(16):
                    for k4 in range(4):
                        p.add("tensor", lambda e, kg=kg, k4=k4: e.transpose(
                            PST[:, k4 * 128:(k4 + 1) * 128],
                            hgrow[:, kg * 512 + k4 * 128: kg * 512 + (k4 + 1) * 128],
                            id_bf[:, :]))
                    p.add("vector", lambda e: e.tensor_copy(gsT[:, :], PST[:, 0:512]))
                    for k4 in range(4):
                        k = kg * 4 + k4
                        for vc in range(NCH):
                            first = (kg == 0 and k4 == 0 and vc in (0, 4))
                            last = (kg == 15 and k4 == 3 and vc in (3, 7))
                            p.add("tensor", lambda e, k=k, k4=k4, vc=vc, first=first, last=last: e.matmul(
                                PSA[0:128, vc * 128:(vc + 1) * 128],
                                vals[:, k * 1024 + vc * 128: k * 1024 + (vc + 1) * 128],
                                gsT[:, k4 * 128:(k4 + 1) * 128],
                                start=first, stop=last, skip_group_check=True))
                p.add("scalar", lambda e: e.activation(
                    gstage[:, 0:512], PSA[0:128, 0:512], AF.Silu))
                p.add("scalar", lambda e: e.activation(
                    gstage[:, 512:1024], PSA[0:128, 512:1024], AF.Silu))
                p.dma("sync",
                      _ap(g_o, tt * 128, [[T_, 128], [128 * T_, NCH], [1, 128]]),
                      _ap(gstage, 0, [[1024, 128], [128, NCH], [1, 128]]))

        p.emit(block, sem)
    return nc


def build_B():
    nc = bass.Bass("TRN2", target_bir_lowering=False)
    qT = nc.dram_tensor("qT", [E_, T_], BF16, kind="ExternalInput")
    kTf = nc.dram_tensor("kTf", [E_, S_], BF16, kind="ExternalInput")
    vgf = nc.dram_tensor("vgf", [4 * S_, 260], BF16, kind="ExternalInput")
    xT = nc.dram_tensor("xT", [E_, T_], F32, kind="ExternalInput")
    g0T = nc.dram_tensor("g0T", [E_, T_], F32, kind="ExternalInput")
    woT = nc.dram_tensor("woT", [E_, E_], F32, kind="ExternalInput")
    wq1T = nc.dram_tensor("wq1T", [E_, E_], F32, kind="ExternalInput")
    wk1T = nc.dram_tensor("wk1T", [E_, E_], F32, kind="ExternalInput")
    wv1T = nc.dram_tensor("wv1T", [E_, E_], F32, kind="ExternalInput")
    kern1 = nc.dram_tensor("kern1", [E_, E_], F32, kind="ExternalInput")
    bias1 = nc.dram_tensor("bias1", [E_, 1], F32, kind="ExternalInput")
    a1 = nc.dram_tensor("a1", [E_, 16], F32, kind="ExternalInput")
    b1 = nc.dram_tensor("b1", [16, E_], F32, kind="ExternalInput")
    gam = nc.dram_tensor("gam", [16, 1], F32, kind="ExternalInput")
    selq = nc.dram_tensor("selq", [16, NCH * 128], F32, kind="ExternalInput")
    onesr = nc.dram_tensor("onesr", [1, 128], F32, kind="ExternalInput")
    onesc = nc.dram_tensor("onesc", [128, 1], F32, kind="ExternalInput")

    x1T_o = nc.dram_tensor("x1T_o", [E_, T_], F32, kind="ExternalOutput")
    qt1_o = nc.dram_tensor("qt1_o", [E_, T_], BF16, kind="ExternalOutput")
    kt1_o = nc.dram_tensor("kt1_o", [E_, T_], BF16, kind="ExternalOutput")
    v1_o = nc.dram_tensor("v1_o", [T_, E_], BF16, kind="ExternalOutput")
    g1_o = nc.dram_tensor("g1_o", [E_, T_], F32, kind="ExternalOutput")
    am_d = nc.dram_tensor("am_d", [128, 1], F32)

    from contextlib import ExitStack
    es = ExitStack()
    with es:
        ones_r = es.enter_context(nc.sbuf_tensor("ones_r", [1, 128], F32))
        ones_c = es.enter_context(nc.sbuf_tensor("ones_c", [128, 1], F32))
        esY = ExitStack()
        esY.__enter__()
        Yn = esY.enter_context(nc.sbuf_tensor("Yn", [128, NCH * T_], BF16))
        PSA = es.enter_context(nc.psum_tensor("PSA", [128, 2048], F32))
        PSB = es.enter_context(nc.psum_tensor("PSB", [128, 1024], F32))
        PSC = es.enter_context(nc.psum_tensor("PSC", [128, 512], F32))
        sem = es.enter_context(nc.semaphore("step"))
        block = es.enter_context(nc.Block())
        p = Prog(nc)
        p.dma("sync", ones_r[:, :], onesr[:, :])
        p.dma("sync", ones_c[:, :], onesc[:, :])

        es1 = ExitStack()
        es1.__enter__()
        Qs = es1.enter_context(nc.sbuf_tensor("Qs", [128, NCH * T_], BF16, side="right"))
        KA = es1.enter_context(nc.sbuf_tensor("KA", [128, S_], BF16, side="right"))
        KB = es1.enter_context(nc.sbuf_tensor("KB", [128, S_], BF16, side="right"))
        Vg = es1.enter_context(nc.sbuf_tensor("Vg", [128, 64 * 260], BF16, side="right"))
        Ebf = es1.enter_context(nc.sbuf_tensor("Ebf", [128, 512], BF16, side="right"))
        rrow = es1.enter_context(nc.sbuf_tensor("rrow", [1, 512], F32, side="right"))
        Rs = es1.enter_context(nc.sbuf_tensor("Rs", [64, 512], F32, side="right"))
        for c in range(NCH):
            p.dma("sync", Qs[:, c * T_:(c + 1) * T_], qT[c * 128:(c + 1) * 128, :])
        for g in range(4):
            p.dma("sync", KA[:, :], kTf[g * 256:g * 256 + 128, :])
            p.dma("sync", KB[:, :], kTf[g * 256 + 128:g * 256 + 256, :])
            p.dma("sync", Vg[:, :],
                  _ap(vgf, g * S_ * 260, [[260, 128], [128 * 260, 64], [1, 260]]))
            for hl in range(4):
                h = 4 * g + hl
                K_t = KA if hl < 2 else KB
                kr = 64 * (hl % 2)
                qr0 = 64 * (h % 2)
                qc0 = (h // 2) * T_
                for ts in range(4):
                    for sb in range(64):
                        p.add("tensor", lambda e, K_t=K_t, kr=kr, sb=sb, qr0=qr0, qc0=qc0, ts=ts: e.matmul(
                            PSA[0:128, 0:512],
                            K_t[kr:kr + 64, sb * 128:(sb + 1) * 128],
                            Qs[qr0:qr0 + 64, qc0 + ts * 512: qc0 + (ts + 1) * 512],
                            start=True, stop=True))
                        p.add("scalar", lambda e: e.activation(
                            Ebf[:, :], PSA[0:128, 0:512], AF.Exp, scale=0.125))
                        p.add("tensor", lambda e, sb=sb, hl=hl: e.matmul(
                            PSB[0:65, 0:512],
                            Vg[:, sb * 260 + hl * 65: sb * 260 + (hl + 1) * 65],
                            Ebf[:, :], start=(sb == 0), stop=(sb == 63)))
                    p.add("vector", lambda e: e.reciprocal(rrow[:, :], PSB[64:65, 0:512]))
                    p.add("tensor", lambda e: e.matmul(
                        PSC[0:64, :], ones_r[0:1, 0:64], rrow[:, :], start=True, stop=True))
                    p.add("scalar", lambda e: e.activation(Rs[:, :], PSC[0:64, :], AF.Copy))
                    p.add("vector", lambda e, qr0=qr0, qc0=qc0, ts=ts: e.tensor_tensor(
                        Yn[qr0:qr0 + 64, qc0 + ts * 512: qc0 + (ts + 1) * 512],
                        PSB[0:64, 0:512], Rs[:, :], ALU.mult))
        es1.__exit__(None, None, None)

        es2 = ExitStack()
        es2.__enter__()
        X1 = es2.enter_context(nc.sbuf_tensor("X1", [128, NCH * T_], F32, side="right"))
        wo_bf = es2.enter_context(nc.sbuf_tensor("wo_bf", [128, NCH * 1024], BF16, side="right"))
        xsl = es2.enter_context(nc.sbuf_tensor("xsl", [128, 512], F32, side="right"))
        gsl = es2.enter_context(nc.sbuf_tensor("gsl", [128, 512], F32, side="right"))
        tsl = es2.enter_context(nc.sbuf_tensor("tsl", [128, 512], F32, side="right"))
        for c in range(NCH):
            p.dma("gpsimd", wo_bf[:, c * 1024:(c + 1) * 1024],
                  woT[c * 128:(c + 1) * 128, :])
        for oc in range(NCH):
            for ts in range(4):
                for c in range(NCH):
                    p.add("tensor", lambda e, oc=oc, ts=ts, c=c: e.matmul(
                        PSA[0:128, 0:512],
                        wo_bf[:, c * 1024 + oc * 128: c * 1024 + oc * 128 + 128],
                        Yn[:, c * T_ + ts * 512: c * T_ + (ts + 1) * 512],
                        start=(c == 0), stop=(c == NCH - 1)))
                p.dma("sync", xsl[:, :], xT[oc * 128:(oc + 1) * 128, ts * 512:(ts + 1) * 512])
                p.dma("sync", gsl[:, :], g0T[oc * 128:(oc + 1) * 128, ts * 512:(ts + 1) * 512])
                p.add("vector", lambda e: e.tensor_tensor(
                    tsl[:, :], PSA[0:128, 0:512], gsl[:, :], ALU.mult))
                sl = lambda t, oc=oc, ts=ts: t[:, oc * T_ + ts * 512: oc * T_ + (ts + 1) * 512]
                p.add("vector", lambda e, oc=oc, ts=ts: e.tensor_tensor(
                    X1[:, oc * T_ + ts * 512: oc * T_ + (ts + 1) * 512],
                    tsl[:, :], xsl[:, :], ALU.add))
                p.dma("sync", x1T_o[oc * 128:(oc + 1) * 128, ts * 512:(ts + 1) * 512],
                      X1[:, oc * T_ + ts * 512: oc * T_ + (ts + 1) * 512])
        esY.__exit__(None, None, None)

        xn1 = es.enter_context(nc.sbuf_tensor("xn1", [128, NCH * T_], BF16))
        es3 = ExitStack()
        es3.__enter__()
        mu1 = es3.enter_context(nc.sbuf_tensor("mu1", [128, T_], F32, side="right"))
        rs1 = es3.enter_context(nc.sbuf_tensor("rs1", [128, T_], F32, side="right"))
        sqs = es3.enter_context(nc.sbuf_tensor("sqs", [128, 512], F32, side="right"))
        row_a = es3.enter_context(nc.sbuf_tensor("row_a", [1, T_], F32, side="right"))
        row_b = es3.enter_context(nc.sbuf_tensor("row_b", [1, T_], F32, side="right"))
        row_c = es3.enter_context(nc.sbuf_tensor("row_c", [1, 512], F32, side="right"))
        row_d = es3.enter_context(nc.sbuf_tensor("row_d", [1, 512], F32, side="right"))
        scr_a = es3.enter_context(nc.sbuf_tensor("scr_a", [128, T_], F32, side="right"))
        scr_b = es3.enter_context(nc.sbuf_tensor("scr_b", [128, T_], F32, side="right"))
        for ts in range(4):
            for c in range(NCH):
                csl = lambda t, c=c, ts=ts: t[:, c * T_ + ts * 512: c * T_ + (ts + 1) * 512]
                p.add("scalar", lambda e, c=c, ts=ts: e.activation(
                    sqs[:, :], X1[:, c * T_ + ts * 512: c * T_ + (ts + 1) * 512], AF.Square))
                p.add("tensor", lambda e, c=c, ts=ts: e.matmul(
                    PSC[0:1, :], ones_c[:, :],
                    X1[:, c * T_ + ts * 512: c * T_ + (ts + 1) * 512],
                    start=(c == 0), stop=(c == NCH - 1)))
                p.add("tensor", lambda e, c=c: e.matmul(
                    PSB[0:1, 0:512], ones_c[:, :], sqs[:, :],
                    start=(c == 0), stop=(c == NCH - 1)))
            sl = slice(ts * 512, (ts + 1) * 512)
            p.add("vector", lambda e, sl=sl: e.tensor_scalar_mul(
                row_a[:, sl], PSC[0:1, :], 1.0 / 1024.0))
            p.add("vector", lambda e: e.tensor_scalar_mul(
                row_c[:, :], PSB[0:1, 0:512], 1.0 / 1024.0))
            p.add("vector", lambda e, sl=sl: e.tensor_tensor(
                row_d[:, :], row_a[:, sl], row_a[:, sl], ALU.mult))
            p.add("vector", lambda e: e.tensor_tensor(
                row_c[:, :], row_c[:, :], row_d[:, :], ALU.subtract))
            p.add("vector", lambda e: e.tensor_scalar_add(row_c[:, :], row_c[:, :], 1e-5))
            p.add("scalar", lambda e: e.activation(row_d[:, :], row_c[:, :], AF.Sqrt))
            p.add("vector", lambda e, sl=sl: e.reciprocal(row_b[:, sl], row_d[:, :]))
            p.add("tensor", lambda e, sl=sl: e.matmul(
                PSB[0:128, 0:512], ones_r[:, :], row_a[:, sl], start=True, stop=True))
            p.add("vector", lambda e, sl=sl: e.tensor_copy(mu1[:, sl], PSB[0:128, 0:512]))
            p.add("tensor", lambda e, sl=sl: e.matmul(
                PSC[0:128, :], ones_r[:, :], row_b[:, sl], start=True, stop=True))
            p.add("vector", lambda e, sl=sl: e.tensor_copy(rs1[:, sl], PSC[0:128, :]))
        for c in range(NCH):
            p.add("vector", lambda e, c=c: e.tensor_tensor(
                scr_a[:, :], X1[:, c * T_:(c + 1) * T_], mu1[:, :], ALU.subtract))
            p.add("vector", lambda e: e.tensor_tensor(
                scr_b[:, :], scr_a[:, :], rs1[:, :], ALU.mult))
            p.add("vector", lambda e, c=c: e.tensor_copy(
                xn1[:, c * T_:(c + 1) * T_], scr_b[:, :]))
        es3.__exit__(None, None, None)
        es2.__exit__(None, None, None)

        es4 = ExitStack()
        es4.__enter__()
        W1 = es4.enter_context(nc.sbuf_tensor("W1", [128, 3 * NCH * 1024], BF16, side="right"))
        U512 = es4.enter_context(nc.sbuf_tensor("U512", [128, T_], F32, side="right"))
        U128 = es4.enter_context(nc.sbuf_tensor("U128", [128, T_], F32, side="right"))
        LG = es4.enter_context(nc.sbuf_tensor("LG", [128, NCH], F32, side="right"))
        NLG = es4.enter_context(nc.sbuf_tensor("NLG", [128, NCH], F32, side="right"))
        lnGr = es4.enter_context(nc.sbuf_tensor("lnGr", [16, 1], F32, side="right"))
        lnG = es4.enter_context(nc.sbuf_tensor("lnG", [16, 1], F32, side="right"))
        selq_s = es4.enter_context(nc.sbuf_tensor("selq_s", [16, NCH * 128], F32, side="right"))
        Pq = es4.enter_context(nc.sbuf_tensor("Pq", [128, T_], F32, side="right"))
        Pk = es4.enter_context(nc.sbuf_tensor("Pk", [128, T_], F32, side="right"))
        scp = es4.enter_context(nc.sbuf_tensor("scp", [128, T_], F32, side="right"))
        qst = es4.enter_context(nc.sbuf_tensor("qst", [128, T_], BF16, side="right"))
        v1st = es4.enter_context(nc.sbuf_tensor("v1st", [128, 1024], BF16, side="right"))
        for w_i, wsrc in enumerate((wq1T, wk1T, wv1T)):
            for c in range(NCH):
                p.dma("gpsimd", W1[:, (w_i * NCH + c) * 1024:(w_i * NCH + c + 1) * 1024],
                      wsrc[c * 128:(c + 1) * 128, :])
        p.dma("sync", selq_s[:, :], selq[:, :])
        p.dma("sync", lnGr[:, :], gam[:, :])
        p.add("scalar", lambda e: e.activation(lnG[:, :], lnGr[:, :], AF.Ln))
        for c in range(NCH):
            p.add("tensor", lambda e, c=c: e.matmul(
                PSC[0:128, 0:1], selq_s[:, c * 128:(c + 1) * 128], lnG[:, :],
                start=True, stop=True))
            p.add("vector", lambda e, c=c: e.tensor_copy(LG[:, c:c + 1], PSC[0:128, 0:1]))
        p.add("vector", lambda e: e.tensor_scalar_mul(NLG[:, :], LG[:, :], -1.0))
        p.add("gpsimd", lambda e: e.iota(U512[:, :], [[0, 4], [1, 512]], base=0,
                                         channel_multiplier=0,
                                         allow_small_or_imprecise_dtypes=True))
        p.add("gpsimd", lambda e: e.iota(U128[:, :], [[0, 16], [1, 128]], base=0,
                                         channel_multiplier=0,
                                         allow_small_or_imprecise_dtypes=True))
        LN8 = float(np.log(0.125))
        for oc in range(NCH):
            lgb = _ap(LG, oc, [[NCH, 128], [0, T_]])
            nlgb = _ap(NLG, oc, [[NCH, 128], [0, T_]])
            p.add("vector", lambda e, lgb=lgb: e.tensor_tensor(scp[:, :], U512[:, :], lgb, ALU.mult))
            p.add("scalar", lambda e: e.activation(Pq[:, :], scp[:, :], AF.Exp, bias=np.float32(LN8)))
            p.add("vector", lambda e, nlgb=nlgb: e.tensor_tensor(scp[:, :], U128[:, :], nlgb, ALU.mult))
            p.add("scalar", lambda e: e.activation(Pk[:, :], scp[:, :], AF.Exp))
            for w_i, dst, P in ((0, qt1_o, Pq), (1, kt1_o, Pk)):
                for ts in range(4):
                    for c in range(NCH):
                        p.add("tensor", lambda e, w_i=w_i, oc=oc, c=c, ts=ts: e.matmul(
                            PSA[0:128, 0:512],
                            W1[:, (w_i * NCH + c) * 1024 + oc * 128:
                                  (w_i * NCH + c) * 1024 + oc * 128 + 128],
                            xn1[:, c * T_ + ts * 512: c * T_ + (ts + 1) * 512],
                            start=(c == 0), stop=(c == NCH - 1)))
                    p.add("vector", lambda e, ts=ts, P=P: e.tensor_tensor(
                        qst[:, ts * 512:(ts + 1) * 512], PSA[0:128, 0:512],
                        P[:, ts * 512:(ts + 1) * 512], ALU.mult))
                p.dma("sync", dst[oc * 128:(oc + 1) * 128, :], qst[:, :])
        for tt in range(16):
            for os in range(2):
                for c in range(NCH):
                    p.add("tensor", lambda e, os=os, c=c, tt=tt: e.matmul(
                        PSB[0:128, os * 512:(os + 1) * 512],
                        xn1[:, c * T_ + tt * 128: c * T_ + (tt + 1) * 128],
                        W1[:, (2 * NCH + c) * 1024 + os * 512:
                              (2 * NCH + c) * 1024 + (os + 1) * 512],
                        start=(c == 0), stop=(c == NCH - 1)))
            p.add("vector", lambda e: e.tensor_copy(v1st[:, :], PSB[0:128, :]))
            p.dma("sync", v1_o[tt * 128:(tt + 1) * 128, :], v1st[:, :])
        es4.__exit__(None, None, None)

        es5 = ExitStack()
        es5.__enter__()
        wq1 = es5.enter_context(nc.sbuf_tensor("wq1", [128, NCH * 1024], BF16, side="right"))
        kch = es5.enter_context(nc.sbuf_tensor("kch", [128, 1024], F32, side="right"))
        tq = es5.enter_context(nc.sbuf_tensor("tq", [128, 1024], F32, side="right"))
        am8 = es5.enter_context(nc.sbuf_tensor("am8", [128, NCH], F32, side="right"))
        amc = es5.enter_context(nc.sbuf_tensor("amc", [128, 1], F32, side="right"))
        amr = es5.enter_context(nc.sbuf_tensor("amr", [1, 128], F32, side="right"))
        am = es5.enter_context(nc.sbuf_tensor("am", [1, 1], F32, side="right"))
        srow = es5.enter_context(nc.sbuf_tensor("srow", [1, 1], F32, side="right"))
        rsrow = es5.enter_context(nc.sbuf_tensor("rsrow", [1, 1], F32, side="right"))
        scol = es5.enter_context(nc.sbuf_tensor("scol", [128, 1], F32, side="right"))
        rscol = es5.enter_context(nc.sbuf_tensor("rscol", [128, 1], F32, side="right"))
        a_bf = es5.enter_context(nc.sbuf_tensor("a_bf", [128, NCH * 16], BF16, side="right"))
        b_bf = es5.enter_context(nc.sbuf_tensor("b_bf", [16, 1024], BF16, side="right"))
        bias_s = es5.enter_context(nc.sbuf_tensor("bias_s", [128, NCH], F32, side="right"))
        u_bf = es5.enter_context(nc.sbuf_tensor("u_bf", [16, T_], BF16, side="right"))
        g1st = es5.enter_context(nc.sbuf_tensor("g1st", [128, 512], F32, side="right"))
        for c in range(NCH):
            p.dma("sync", kch[:, :], kern1[c * 128:(c + 1) * 128, :])
            p.add("vector", lambda e, c=c: e.reduce_max(
                am8[:, c:c + 1], kch[:, :], AX.X, apply_absolute_value=True))
        p.add("vector", lambda e: e.reduce_max(amc[:, :], am8[:, :], AX.X))
        p.dma("sync", am_d[:, :], amc[:, :])
        p.dma("sync", amr[:, :], _ap(am_d, 0, [[128, 1], [1, 128]]))
        p.add("vector", lambda e: e.reduce_max(am[:, :], amr[:, :], AX.X))
        p.add("vector", lambda e: e.tensor_scalar_mul(srow[:, :], am[:, :], 1.0 / 7.0))
        p.add("vector", lambda e: e.reciprocal(rsrow[:, :], srow[:, :]))
        p.add("tensor", lambda e: e.matmul(PSC[0:128, 0:1], ones_r[:, :], srow[:, :],
                                           start=True, stop=True))
        p.add("vector", lambda e: e.tensor_copy(scol[:, :], PSC[0:128, 0:1]))
        p.add("tensor", lambda e: e.matmul(PSC[0:128, 0:1], ones_r[:, :], rsrow[:, :],
                                           start=True, stop=True))
        p.add("vector", lambda e: e.tensor_copy(rscol[:, :], PSC[0:128, 0:1]))
        for c in range(NCH):
            p.dma("sync", kch[:, :], kern1[c * 128:(c + 1) * 128, :])
            p.add("vector", lambda e: e.tensor_scalar(
                tq[:, :], kch[:, :], rscol[:, 0:1], MAGIC, ALU.mult, ALU.add))
            p.add("vector", lambda e: e.tensor_scalar(
                kch[:, :], tq[:, :], MAGIC, -8.0, ALU.subtract, ALU.max))
            p.add("vector", lambda e, c=c: e.tensor_scalar(
                wq1[:, c * 1024:(c + 1) * 1024], kch[:, :], 7.0, scol[:, 0:1],
                ALU.min, ALU.mult))
        for c in range(NCH):
            p.dma("gpsimd", a_bf[:, c * 16:(c + 1) * 16], a1[c * 128:(c + 1) * 128, :])
            p.dma("sync", bias_s[:, c:c + 1], bias1[c * 128:(c + 1) * 128, :])
        p.dma("gpsimd", b_bf[:, :], b1[:, :])
        for ts in range(4):
            for c in range(NCH):
                p.add("tensor", lambda e, c=c, ts=ts: e.matmul(
                    PSC[0:16, :], a_bf[:, c * 16:(c + 1) * 16],
                    xn1[:, c * T_ + ts * 512: c * T_ + (ts + 1) * 512],
                    start=(c == 0), stop=(c == NCH - 1)))
            p.add("vector", lambda e, ts=ts: e.tensor_copy(
                u_bf[:, ts * 512:(ts + 1) * 512], PSC[0:16, :]))
        for fc in range(NCH):
            for ts in range(4):
                for c in range(NCH):
                    p.add("tensor", lambda e, fc=fc, ts=ts, c=c: e.matmul(
                        PSA[0:128, 0:512],
                        wq1[:, c * 1024 + fc * 128: c * 1024 + fc * 128 + 128],
                        xn1[:, c * T_ + ts * 512: c * T_ + (ts + 1) * 512],
                        start=(c == 0), stop=False))
                p.add("tensor", lambda e, fc=fc, ts=ts: e.matmul(
                    PSA[0:128, 0:512], b_bf[:, fc * 128:(fc + 1) * 128],
                    u_bf[:, ts * 512:(ts + 1) * 512], start=False, stop=True))
                p.add("scalar", lambda e, fc=fc: e.activation(
                    g1st[:, :], PSA[0:128, 0:512], AF.Silu, bias=bias_s[:, fc:fc + 1]))
                p.dma("sync", g1_o[fc * 128:(fc + 1) * 128, ts * 512:(ts + 1) * 512],
                      g1st[:, :])
        es5.__exit__(None, None, None)
        p.emit(block, sem)
    return nc


def build_D():
    nc = bass.Bass("TRN2", target_bir_lowering=False)
    qtg = nc.dram_tensor("qtg", [256, S_], BF16, kind="ExternalInput")
    ktg = nc.dram_tensor("ktg", [256, S_], BF16, kind="ExternalInput")
    vg1 = nc.dram_tensor("vg1", [S_, 256], BF16, kind="ExternalInput")
    gamh = nc.dram_tensor("gamh", [4, 1], F32, kind="ExternalInput")
    mgrid = nc.dram_tensor("mgrid", [1, 4 * 67], F32, kind="ExternalInput")
    onesr = nc.dram_tensor("onesr", [1, 128], F32, kind="ExternalInput")
    y1T_o = nc.dram_tensor("y1T_o", [256, S_], BF16, kind="ExternalOutput")

    from contextlib import ExitStack
    es = ExitStack()
    with es:
        qtA = es.enter_context(nc.sbuf_tensor("qtA", [128, S_], BF16))
        qtB = es.enter_context(nc.sbuf_tensor("qtB", [128, S_], BF16))
        ktA = es.enter_context(nc.sbuf_tensor("ktA", [128, S_], BF16))
        ktB = es.enter_context(nc.sbuf_tensor("ktB", [128, S_], BF16))
        Vs = es.enter_context(nc.sbuf_tensor("Vs", [128, 64 * 256], BF16))
        y1s = es.enter_context(nc.sbuf_tensor("y1s", [128, 2 * S_], BF16))
        ones_r = es.enter_context(nc.sbuf_tensor("ones_r", [1, 128], F32))
        mg = es.enter_context(nc.sbuf_tensor("mg", [1, 4 * 67], F32))
        gr = es.enter_context(nc.sbuf_tensor("gr", [1, 4], F32))
        lnr = es.enter_context(nc.sbuf_tensor("lnr", [1, 4], F32))
        erow = es.enter_context(nc.sbuf_tensor("erow", [1, 4 * 67], F32))
        crow = es.enter_context(nc.sbuf_tensor("crow", [1, 4 * 67], F32))
        ctab = es.enter_context(nc.sbuf_tensor("ctab", [128, 4 * 67], F32))
        UD = es.enter_context(nc.sbuf_tensor("UD", [128, 512], F32))
        M01 = es.enter_context(nc.sbuf_tensor("M01", [128, 512], F32))
        Wbf = es.enter_context(nc.sbuf_tensor("Wbf", [128, 512], BF16))
        PSA = es.enter_context(nc.psum_tensor("PSA", [128, 512], F32))
        PSB = es.enter_context(nc.psum_tensor("PSB", [128, 512], F32))
        sem = es.enter_context(nc.semaphore("step"))
        block = es.enter_context(nc.Block())
        p = Prog(nc)

        p.dma("sync", ones_r[:, :], onesr[:, :])
        p.dma("sync", mg[:, :], mgrid[:, :])
        p.dma("sync", gr[:, :], _ap(gamh, 0, [[4, 1], [1, 4]]))
        p.dma("sync", qtA[:, :], qtg[0:128, :])
        p.dma("sync", qtB[:, :], qtg[128:256, :])
        p.dma("sync", ktA[:, :], ktg[0:128, :])
        p.dma("sync", ktB[:, :], ktg[128:256, :])
        p.dma("sync", Vs[:, :], _ap(vg1, 0, [[256, 128], [128 * 256, 64], [1, 256]]))
        p.add("scalar", lambda e: e.activation(lnr[:, :], gr[:, :], AF.Ln))
        lnb = _ap(lnr, 0, [[4, 1], [1, 4], [0, 67]])
        p.add("vector", lambda e, lnb=lnb: e.tensor_tensor(erow[:, :], mg[:, :], lnb, ALU.mult))
        p.add("scalar", lambda e: e.activation(crow[:, :], erow[:, :], AF.Exp))
        p.add("tensor", lambda e: e.matmul(PSB[0:128, 0:268], ones_r[:, :], crow[:, :],
                                           start=True, stop=True))
        p.add("vector", lambda e: e.tensor_copy(ctab[:, :], PSB[0:128, 0:268]))
        p.add("gpsimd", lambda e: e.iota(UD[:, :], [[1, 512]], base=0,
                                         channel_multiplier=-1,
                                         allow_small_or_imprecise_dtypes=True))
        p.add("vector", lambda e: e.tensor_single_scalar(M01[:, :], UD[:, :], 0.0, ALU.is_ge))

        for hl in range(4):
            kt = ktA if hl < 2 else ktB
            qt = qtA if hl < 2 else qtB
            r0 = 64 * (hl % 2)
            for ts in range(16):
                nsb = 4 * ts + 4
                for sb in range(nsb):
                    p.add("tensor", lambda e, kt=kt, qt=qt, r0=r0, sb=sb, ts=ts: e.matmul(
                        PSA[:, :], kt[r0:r0 + 64, sb * 128:(sb + 1) * 128],
                        qt[r0:r0 + 64, ts * 512:(ts + 1) * 512], start=True, stop=True))
                    if sb < 4 * ts:
                        idx = hl * 67 + (4 * ts - sb) + 3
                        p.add("vector", lambda e, idx=idx: e.tensor_scalar_mul(
                            Wbf[:, :], PSA[:, :], ctab[:, idx:idx + 1]))
                    else:
                        j = sb - 4 * ts
                        idx = hl * 67 + 3 - j
                        if j > 0:
                            p.add("vector", lambda e, j=j: e.memset(Wbf[:, 0:j * 128], 0.0))
                        p.add("vector", lambda e, j=j, idx=idx: e.scalar_tensor_tensor(
                            Wbf[:, j * 128:512], PSA[:, j * 128:512], ctab[:, idx:idx + 1],
                            M01[:, 0:512 - j * 128], ALU.mult, ALU.mult))
                    p.add("tensor", lambda e, sb=sb, hl=hl, nsb=nsb: e.matmul(
                        PSB[0:64, :], Vs[:, sb * 256 + hl * 64: sb * 256 + (hl + 1) * 64],
                        Wbf[:, :], start=(sb == 0), stop=(sb == nsb - 1)))
                p.add("vector", lambda e, r0=r0, hl=hl, ts=ts: e.tensor_copy(
                    y1s[r0:r0 + 64, (hl // 2) * S_ + ts * 512: (hl // 2) * S_ + (ts + 1) * 512],
                    PSB[0:64, :]))
        p.dma("sync", y1T_o[0:128, :], y1s[:, 0:S_])
        p.dma("sync", y1T_o[128:256, :], y1s[:, S_:2 * S_])
        p.emit(block, sem)
    return nc


def build_E():
    nc = bass.Bass("TRN2", target_bir_lowering=False)
    y1T = nc.dram_tensor("y1T", [E_, T_], BF16, kind="ExternalInput")
    wo1T = nc.dram_tensor("wo1T", [E_, E_], F32, kind="ExternalInput")
    x1T = nc.dram_tensor("x1T", [E_, T_], F32, kind="ExternalInput")
    g1T = nc.dram_tensor("g1T", [E_, T_], F32, kind="ExternalInput")
    x2T_o = nc.dram_tensor("x2T_o", [E_, T_], F32, kind="ExternalOutput")

    from contextlib import ExitStack
    es = ExitStack()
    with es:
        y1_sb = es.enter_context(nc.sbuf_tensor("y1_sb", [128, NCH * T_], BF16))
        wo1_bf = es.enter_context(nc.sbuf_tensor("wo1_bf", [128, NCH * 1024], BF16))
        xsl = es.enter_context(nc.sbuf_tensor("xsl", [128, 512], F32))
        gsl = es.enter_context(nc.sbuf_tensor("gsl", [128, 512], F32))
        tsl = es.enter_context(nc.sbuf_tensor("tsl", [128, 512], F32))
        osl = es.enter_context(nc.sbuf_tensor("osl", [128, 512], F32))
        PSA = es.enter_context(nc.psum_tensor("PSA", [128, 512], F32))
        sem = es.enter_context(nc.semaphore("step"))
        block = es.enter_context(nc.Block())
        p = Prog(nc)
        for c in range(NCH):
            p.dma("sync", y1_sb[:, c * T_:(c + 1) * T_], y1T[c * 128:(c + 1) * 128, :])
            p.dma("gpsimd", wo1_bf[:, c * 1024:(c + 1) * 1024],
                  wo1T[c * 128:(c + 1) * 128, :])
        for oc in range(NCH):
            for ts in range(4):
                for c in range(NCH):
                    p.add("tensor", lambda e, oc=oc, ts=ts, c=c: e.matmul(
                        PSA[:, :],
                        wo1_bf[:, c * 1024 + oc * 128: c * 1024 + oc * 128 + 128],
                        y1_sb[:, c * T_ + ts * 512: c * T_ + (ts + 1) * 512],
                        start=(c == 0), stop=(c == NCH - 1)))
                p.dma("sync", xsl[:, :], x1T[oc * 128:(oc + 1) * 128, ts * 512:(ts + 1) * 512])
                p.dma("sync", gsl[:, :], g1T[oc * 128:(oc + 1) * 128, ts * 512:(ts + 1) * 512])
                p.add("vector", lambda e: e.tensor_tensor(
                    tsl[:, :], PSA[:, :], gsl[:, :], ALU.mult))
                p.add("vector", lambda e: e.tensor_tensor(
                    osl[:, :], tsl[:, :], xsl[:, :], ALU.add))
                p.dma("sync", x2T_o[oc * 128:(oc + 1) * 128, ts * 512:(ts + 1) * 512],
                      osl[:, :])
        p.emit(block, sem)
    return nc


def _rope_tables_host():
    import jax.numpy as jnp
    with jax.default_device(jax.devices("cpu")[0]):
        idx = jnp.arange(HD_ // 2, dtype=jnp.float32)
        ang = jnp.power(jnp.asarray(10000.0, jnp.float32), 2.0 * idx / HD_)
        th = jnp.arange(S_, dtype=jnp.float32)[:, None] * ang[None, :]
        return np.asarray(jnp.cos(th)), np.asarray(jnp.sin(th))


def kernel(**inputs):
    asc = np.ascontiguousarray
    cores = list(range(NCORE))

    cos, sin = _rope_tables_host()
    idf = np.eye(128, dtype=np.float32)
    onesr = np.ones((1, 128), np.float32)
    onesc = np.ones((128, 1), np.float32)
    selq = np.zeros((16, 1024), np.float32)
    for c in range(NCH):
        selq[2 * c, c * 128:c * 128 + 64] = 1.0
        selq[2 * c + 1, c * 128 + 64:(c + 1) * 128] = 1.0
    mgrid = np.concatenate(
        [128.0 * np.arange(-3, 64, dtype=np.float32)[None, :]] * 4, axis=1)

    x = np.asarray(inputs["L0_attn_x"], np.float32)
    wqT = asc(inputs["L0_attn_w_q"].T)
    wkT = asc(inputs["L0_attn_w_k"].T)
    wvT = asc(inputs["L0_attn_w_v"].T)
    woT = asc(inputs["L0_attn_w_o"].T)
    selT = asc(inputs["L0_ffn_expert_sel"].T)
    keys = np.asarray(inputs["L0_ffn_keys"], np.float32)
    values = np.asarray(inputs["L0_ffn_values"], np.float32)
    wq1T = asc(inputs["L1_attn_wq"].T)
    wk1T = asc(inputs["L1_attn_wk"].T)
    wv1T = asc(inputs["L1_attn_wv"].T)
    wo1T = asc(inputs["L1_attn_wo"].T)
    gammas = np.asarray(inputs["L1_attn_gammas"], np.float32)
    kern1 = np.asarray(inputs["L1_ffn_kernel"], np.float32)
    bias1 = asc(inputs["L1_ffn_bias"].reshape(E_, 1))
    a1 = np.asarray(inputs["L1_ffn_a"], np.float32)
    b1 = np.asarray(inputs["L1_ffn_b"], np.float32)

    def core_bt(c):
        return c // 4, 2048 * (c % 4)

    in_a = []
    for c in cores:
        b, t0 = core_bt(c)
        sl = slice(t0, t0 + T_)
        in_a.append(dict(xT=asc(x[b, sl].T), wqT=wqT, wkT=wkT, wvT=wvT,
                         selT=selT, keys=keys, values=values,
                         cosL=asc(cos[sl]), sinL=asc(sin[sl]),
                         idf=idf, onesr=onesr, onesc=onesc))
    ra = run_bass_kernel_spmd(build_A(), in_a, core_ids=cores).results

    ones_col = np.ones((S_, 1), ml_dtypes.bfloat16)
    kTf, vgf = {}, {}
    for b in range(2):
        grp = [b * 4 + i for i in range(4)]
        kTf[b] = asc(np.concatenate([np.asarray(ra[c]["kT_o"]) for c in grp], axis=1))
        vfull = np.concatenate([np.asarray(ra[c]["v_o"]) for c in grp], axis=0)
        blocks = []
        for g in range(4):
            cols = []
            for hl in range(4):
                h = 4 * g + hl
                cols.append(vfull[:, h * 64:(h + 1) * 64])
                cols.append(ones_col)
            blocks.append(np.concatenate(cols, axis=1))
        vgf[b] = asc(np.concatenate(blocks, axis=0))

    in_b = []
    for c in cores:
        b, t0 = core_bt(c)
        sl = slice(t0, t0 + T_)
        in_b.append(dict(qT=np.asarray(ra[c]["qT_o"]), kTf=kTf[b], vgf=vgf[b],
                         xT=in_a[c]["xT"], g0T=np.asarray(ra[c]["g_o"]),
                         woT=woT, wq1T=wq1T, wk1T=wk1T, wv1T=wv1T,
                         kern1=kern1, bias1=bias1, a1=a1, b1=b1,
                         gam=asc(gammas.reshape(16, 1)), selq=selq,
                         onesr=onesr, onesc=onesc))
    rb = run_bass_kernel_spmd(build_B(), in_b, core_ids=cores).results

    qtT, ktT, v1full = {}, {}, {}
    for b in range(2):
        grp = [b * 4 + i for i in range(4)]
        qtT[b] = np.concatenate([np.asarray(rb[c]["qt1_o"]) for c in grp], axis=1)
        ktT[b] = np.concatenate([np.asarray(rb[c]["kt1_o"]) for c in grp], axis=1)
        v1full[b] = np.concatenate([np.asarray(rb[c]["v1_o"]) for c in grp], axis=0)
    in_d = []
    for c in cores:
        b, hg0 = c // 4, 4 * (c % 4)
        in_d.append(dict(qtg=asc(qtT[b][hg0 * 64:(hg0 + 4) * 64, :]),
                         ktg=asc(ktT[b][hg0 * 64:(hg0 + 4) * 64, :]),
                         vg1=asc(v1full[b][:, hg0 * 64:(hg0 + 4) * 64]),
                         gamh=asc(gammas[hg0:hg0 + 4].reshape(4, 1)),
                         mgrid=mgrid, onesr=onesr))
    rd = run_bass_kernel_spmd(build_D(), in_d, core_ids=cores).results

    in_e = []
    for c in cores:
        b, t0 = core_bt(c)
        rows = []
        for h in range(16):
            oc = b * 4 + h // 4
            rows.append(np.asarray(rd[oc]["y1T_o"])[64 * (h % 4):64 * (h % 4) + 64,
                                                    t0:t0 + T_])
        in_e.append(dict(y1T=asc(np.concatenate(rows, axis=0)), wo1T=wo1T,
                         x1T=np.asarray(rb[c]["x1T_o"]),
                         g1T=np.asarray(rb[c]["g1_o"])))
    re_ = run_bass_kernel_spmd(build_E(), in_e, core_ids=cores).results

    out = np.empty((B_, S_, E_), np.float32)
    for c in cores:
        b, t0 = core_bt(c)
        out[b, t0:t0 + T_, :] = np.asarray(re_[c]["x2T_o"]).T
    return out
